# version 15
# baseline (speedup 1.0000x reference)
import numpy as np
import ml_dtypes
from contextlib import ExitStack

import concourse.bass as bass
import concourse.mybir as mybir
from concourse.bass_utils import run_bass_kernel_spmd

F32 = mybir.dt.float32
BF16 = mybir.dt.bfloat16
AF = mybir.ActivationFunctionType
ALU = mybir.AluOpType
AX = mybir.AxisListType

NCORE = 8
D = 1024
H = 16
HD = 64
NSEG = 4
SEG = 1024
NGRP = 2
GRP = 512
NTIL = 4
NE = 64
EC = 128
NWIN = 2
WIN = 512
EPS_LN = 1e-5
EPS_ATT = 1e-6
MODE = "full"
SKIP = set()
SECMAP = {}
_CURSEC = [""]


def build_program():
    nc = bass.Bass("TRN2", target_bir_lowering=False)

    x_d = nc.dram_tensor("x", [4096, D], F32, kind="ExternalInput").ap()
    wq_d = nc.dram_tensor("wq", [D, D], F32, kind="ExternalInput").ap()
    wk_d = nc.dram_tensor("wk", [D, D], F32, kind="ExternalInput").ap()
    wv_d = nc.dram_tensor("wv", [D, D], F32, kind="ExternalInput").ap()
    wo_d = nc.dram_tensor("wo", [D, D], F32, kind="ExternalInput").ap()
    bq_d = nc.dram_tensor("bq", [1, D], F32, kind="ExternalInput").ap()
    bk_d = nc.dram_tensor("bk", [1, D], F32, kind="ExternalInput").ap()
    bv_d = nc.dram_tensor("bv", [1, D], F32, kind="ExternalInput").ap()
    bo_d = nc.dram_tensor("bo", [1, D], F32, kind="ExternalInput").ap()
    sel_d = nc.dram_tensor("sel", [NE, D], F32, kind="ExternalInput").ap()
    keys_d = nc.dram_tensor("keys", [NE, D, EC], F32, kind="ExternalInput").ap()
    vals_d = nc.dram_tensor("vals", [NE, EC, D], F32, kind="ExternalInput").ap()
    out_d = nc.dram_tensor("out", [4096, D], F32, kind="ExternalOutput").ap()

    id128_np = np.eye(128, dtype=ml_dtypes.bfloat16)
    id64_np = np.eye(64, dtype=ml_dtypes.bfloat16)
    mask_np = np.triu(np.ones((128, 128), dtype=np.float32))
    id128_d = nc.inline_tensor(id128_np, name="id128c").ap()
    id64_d = nc.inline_tensor(id64_np, name="id64c").ap()
    mask_d = nc.inline_tensor(mask_np, name="maskc").ap()

    ctx = ExitStack()
    e = ctx.enter_context

    id128 = e(nc.sbuf_tensor("id128", [128, 128], BF16))
    id64 = e(nc.sbuf_tensor("id64", [64, 64], BF16))
    mask = e(nc.sbuf_tensor("mask", [128, 128], F32))
    selT = e(nc.sbuf_tensor("selT", [128, 8, NE], BF16))
    ones_row = e(nc.sbuf_tensor("ones_row", [1, 128], F32))
    ones_bf = e(nc.sbuf_tensor("ones_bf", [1, 128], BF16))
    bo_bf = e(nc.sbuf_tensor("bo_bf", [1, D], BF16))
    one1 = e(nc.sbuf_tensor("one1", [1, 1], F32))
    bo_row = e(nc.sbuf_tensor("bo_row", [1, D], F32))
    sq_dump = e(nc.sbuf_tensor("sq_dump", [128, D], BF16))
    bq_cols = e(nc.sbuf_tensor("bq_cols", [64, 16], F32))
    bk_cols = e(nc.sbuf_tensor("bk_cols", [64, 16], F32))
    bqk_cols = e(nc.sbuf_tensor("bqk_cols", [128, 16], F32))
    bv_bc = e(nc.sbuf_tensor("bv_bc", [128, D], F32))
    wv_sb = e(nc.sbuf_tensor("wv_sb", [128, 8, D], BF16))
    wo_sb = e(nc.sbuf_tensor("wo_sb", [128, 8, D], BF16))
    S_f32 = e(nc.sbuf_tensor("S_f32", [64, H, HD + 1], F32))
    S_bf = e(nc.sbuf_tensor("S_bf", [64, H, HD + 1], BF16))
    y_tok = [e(nc.sbuf_tensor(f"y_tok{i}", [128, D], F32)) for i in range(8)]
    colsA = e(nc.sbuf_tensor("colsA", [128, 16], F32))
    eps_col = e(nc.sbuf_tensor("eps_col", [128, 1], F32))
    nhalf_col = e(nc.sbuf_tensor("nhalf_col", [128, 1], F32))
    den_sb = e(nc.sbuf_tensor("den_sb", [128, 16], F32))
    rcp_sb = e(nc.sbuf_tensor("rcp_sb", [128, 16], F32))

    pb = [e(nc.psum_tensor(f"pb{i}", [128, 512], F32)) for i in range(8)]

    actxW = ExitStack()
    wqk_sb = actxW.enter_context(
        nc.sbuf_tensor("wqk_sb", [128, 8, H, 128], BF16))

    actxA = ExitStack()
    a = actxA.enter_context
    actxS = ExitStack()
    s_ = actxS.enter_context
    b_rows = s_(nc.sbuf_tensor("b_rows", [1, 3 * D], F32))
    sel_bf = s_(nc.sbuf_tensor("sel_bf", [NE, D], BF16))
    actxS.close()
    x_tok = [a(nc.sbuf_tensor(f"x_tok{i}", [128, D], F32)) for i in range(4)]
    x_bf = [a(nc.sbuf_tensor(f"x_bf{i}", [128, D], BF16)) for i in range(2)]
    qT = a(nc.sbuf_tensor("qT", [64, H, GRP], BF16))
    kT = a(nc.sbuf_tensor("kT", [64, H, GRP], BF16))
    v_aug = [a(nc.sbuf_tensor(f"v_aug{i}", [128, H * (HD + 1)], BF16)) for i in range(4)]
    k_tok = [a(nc.sbuf_tensor(f"k_tok{i}", [128, H, HD], BF16)) for i in range(4)]
    exp_t = [a(nc.sbuf_tensor(f"exp_t{i}", [128, GRP], F32)) for i in range(2)]
    t_relu = a(nc.sbuf_tensor("t_relu", [128, GRP], F32))
    r_sb = [a(nc.sbuf_tensor(f"r_sb{i}", [128, D], F32)) for i in range(2)]
    actxA1 = ExitStack()
    xT = actxA1.enter_context(nc.sbuf_tensor("xT", [128, 8, GRP], BF16))
    actxA1.close()
    actxA2 = ExitStack()
    a2_ = actxA2.enter_context
    PT = a2_(nc.sbuf_tensor("PT", [128, H * 128], BF16))
    attn_tok = a2_(nc.sbuf_tensor("attn_tok", [128, H * HD], BF16))
    attnT = a2_(nc.sbuf_tensor("attnT", [128, 8, 128], BF16))
    actxA2.close()
    actxA.close()
    actxW.close()

    actxB = ExitStack()
    b = actxB.enter_context
    yT = b(nc.sbuf_tensor("yT", [128, 8, SEG], BF16))
    G_seg = b(nc.sbuf_tensor("G_seg", [NE, SEG], BF16))
    scores = b(nc.sbuf_tensor("scores", [128, NE, WIN], BF16))
    kbuf = [b(nc.sbuf_tensor(f"kbuf{i}", [128, 8, EC], BF16)) for i in range(4)]
    vbuf = [b(nc.sbuf_tensor(f"vbuf{i}", [128, D], BF16)) for i in range(4)]
    relu_sb = [b(nc.sbuf_tensor(f"relu_sb{i}", [128, WIN], BF16)) for i in range(2)]
    y_bf = [b(nc.sbuf_tensor(f"y_bf{i}", [128, D], BF16)) for i in range(2)]
    s_sig = b(nc.sbuf_tensor("s_sig", [128, NE], F32))
    m8 = b(nc.sbuf_tensor("m8", [128, 8], F32))
    zap = b(nc.sbuf_tensor("zap", [128, NE], F32))
    gates_bf = b(nc.sbuf_tensor("gates_bf", [128, NE], BF16))
    gbuf_til = [b(nc.sbuf_tensor(f"gbuf{i}", [128, NE], BF16))
                for i in range(8)]
    r2_sb = [b(nc.sbuf_tensor(f"r2_sb{i}", [128, D], F32)) for i in range(2)]
    out_f32 = [b(nc.sbuf_tensor(f"out_f32{i}", [128, D], F32)) for i in range(2)]
    actxB.close()

    sem_names = (["setup", "setup2", "pe", "dve", "act",
                  "xb0", "xb1", "o0", "o1"]
                 + [f"x{i}" for i in range(4)]
                 + [f"k{i}" for i in range(4)] + [f"v{i}" for i in range(4)])
    sems = {n: ctx.enter_context(nc.semaphore(f"sem_{n}")) for n in sem_names}
    C = {n: 0 for n in sem_names}

    def flow(eng, E):
        for k in C:
            C[k] = 0

        def I(ek, method, *args, waits=(), inc=None, incn=1, **kw):
            val = None
            if inc is not None:
                if method == "dma_start":
                    incn = 16
                C[inc] += incn
                val = C[inc]
            if E == ek:
                for s2, v2 in waits:
                    eng.wait_ge(sems[s2], v2)
                ins = getattr(eng, method)(*args, **kw)
                if inc is not None:
                    ins.then_inc(sems[inc], incn)
                try:
                    SECMAP[ins.ins.name] = _CURSEC[0]
                except Exception:
                    pass
            return val

        def SEC(name):
            _CURSEC[0] = name

        def W(ek, waits):
            if E == ek and waits:
                for s2, v2 in waits:
                    eng.wait_ge(sems[s2], v2)

        def DR(ek):
            if E == ek:
                eng.drain()

        def src_ap(handle, off, ap):
            return bass.AP(handle, off, ap)

        SEC('setup')
        I("g", "dma_start", id128[:], id128_d, inc="setup")
        I("g", "dma_start", id64[:], id64_d, inc="setup")
        I("g", "dma_start", mask[:], mask_d, inc="setup")
        I("g", "dma_start", sel_bf[:], sel_d, inc="setup")
        for ai in range(8):
            I("g", "dma_start", wqk_sb[:, ai, :, 0:64],
              wq_d.rearrange("(a p) (h c) -> p a h c", p=128, c=64)[:, ai],
              inc="setup")
            I("g", "dma_start", wqk_sb[:, ai, :, 64:128],
              wk_d.rearrange("(a p) (h c) -> p a h c", p=128, c=64)[:, ai],
              inc="setup")
        I("g", "dma_start", wv_sb[:], wv_d.rearrange("(a p) m -> p a m", p=128), inc="setup")
        I("g", "dma_start", wo_sb[:], wo_d.rearrange("(a p) m -> p a m", p=128), inc="setup")
        I("s", "dma_start", b_rows[:, 0:D], bq_d, inc="setup2")
        I("s", "dma_start", b_rows[:, D:2 * D], bk_d, inc="setup2")
        I("s", "dma_start", b_rows[:, 2 * D:3 * D], bv_d, inc="setup2")
        I("s", "dma_start", bo_row[:], bo_d, inc="setup2")
        n_setup = C["setup"]
        n_setup2 = C["setup2"]

        I("v", "memset", ones_row[:], 1.0)
        I("v", "memset", ones_bf[:], 1.0)
        I("v", "memset", eps_col[:], EPS_LN)
        I("v", "memset", nhalf_col[:], -0.5)
        I("v", "memset", one1[:], 1.0)
        I("v", "memset", S_f32[:], 0.0)
        I("v", "memset", S_bf[:], 0.0, inc="dve")
        dve_memset = C["dve"]

        setup_w = [("setup", n_setup), ("setup2", n_setup2), ("dve", dve_memset)]
        for bi, off in [(0, 0), (1, D)]:
            for ai in range(16):
                I("t", "matmul", pb[bi][0:64, ai:ai + 1],
                  b_rows[0:1, off + 64 * ai:off + 64 * (ai + 1)], one1[:],
                  start=True, stop=True,
                  waits=setup_w if (ai == 0 and bi == 0) else ())
        for hf in range(2):
            I("t", "matmul", pb[2 + hf][:, 0:512],
              ones_row[:], b_rows[0:1, 2 * D + 512 * hf:2 * D + 512 * (hf + 1)],
              start=True, stop=True)
        pb4_bf = pb[4][:].bitcast(BF16)
        for ai in range(8):
            I("t", "transpose", pb4_bf[0:128, 128 * ai:128 * ai + 64],
              sel_bf[:, 128 * ai:128 * (ai + 1)], id64[:],
              inc="pe" if ai == 7 else None)
        pe_setup = C["pe"]
        I("v", "tensor_copy", bo_bf[:], bo_row[:],
          waits=[("setup2", n_setup2)])
        I("v", "tensor_copy", bqk_cols[0:64, :], pb[0][0:64, 0:16],
          waits=[("pe", pe_setup)])
        I("v", "tensor_copy", bqk_cols[64:128, :], pb[1][0:64, 0:16])
        I("v", "tensor_copy", bv_bc[:, 0:512], pb[2][:, 0:512])
        I("v", "tensor_copy", bv_bc[:, 512:1024], pb[3][:, 0:512])
        I("v", "tensor_copy", selT[:],
          pb4_bf.rearrange("p (a e) -> p a e", e=128)[:, :, 0:NE], inc="dve")
        dve_setup = C["dve"]

        last_r = {}
        last_xtr = {}
        prev_div = 0
        prev_Supd = dve_setup
        prev_tr_evac = dve_setup
        prev_mask = [dve_setup, dve_setup]
        prev_r_done = dve_setup
        prev_y = [0, 0]
        phaseB_end = None
        prev_flush_dve = 0
        prev_h_tail = None
        prev_out_tail = None
        out_dma = [0, 0]

        for seg in range(NSEG):
            if seg == 0:
                gateA = [("setup", n_setup), ("setup2", n_setup2), ("dve", dve_setup)]
            else:
                gateA = [("pe", phaseB_end["pe"]), ("dve", phaseB_end["dve"]),
                         ("act", phaseB_end["act"]),
                         ("o0", C["o0"]), ("o1", C["o1"])]
                for ai in range(8):
                    I("g", "dma_start", wqk_sb[:, ai, :, 0:64],
                      wq_d.rearrange("(a p) (h c) -> p a h c",
                                     p=128, c=64)[:, ai],
                      waits=[("pe", prev_out_tail[1])] if ai == 0 else (),
                      inc="setup")
                    I("g", "dma_start", wqk_sb[:, ai, :, 64:128],
                      wk_d.rearrange("(a p) (h c) -> p a h c",
                                     p=128, c=64)[:, ai],
                      waits=[("pe", prev_out_tail[1])] if ai == 0 else (),
                      inc="setup")
            wqk_loaded = C["setup"]

            for i2 in range(4):
                I("v", "memset",
                  v_aug[i2][:].rearrange("p (h m) -> p h m", m=HD + 1)[:, :, HD:HD + 1],
                  1.0,
                  waits=(gateA if seg == 0 else [("pe", phaseB_end["pe"])])
                  if i2 == 0 else ())
            if seg % 2 == 0:
                I("v", "memset", S_f32[:], 0.0)
            vaug_init = I("v", "memset",
                          S_bf[:] if seg % 2 == 0 else S_bf[0:1, 0, 0:1],
                          0.0, inc="dve")

            gateX = gateA if seg == 0 else [("pe", phaseB_end["pe"])]
            for g in range(NGRP):
                tiles = [g * NTIL + i2 for i2 in range(NTIL)]
                SEC('xload')
                for til in tiles:
                    slot = til % 4
                    pk = (seg, til - NTIL) if g != 0 else (seg - 1, til + NTIL)
                    lw = list(gateX)
                    if pk in last_r:
                        lw.append(("dve", last_r[pk]))
                    r0 = seg * SEG + til * 128
                    nx = I("s" if til % 2 == 0 else "g", "dma_start",
                           x_tok[slot][:], x_d[r0:r0 + 128, :],
                           waits=lw, inc=f"x{slot}")
                    slot2 = til % 2
                    pk2 = (seg, til - 2) if til >= 2 else (seg - 1, til + 6)
                    lwb = [(f"x{slot}", nx)]
                    if pk2 in last_xtr:
                        lwb.append(("pe", last_xtr[pk2]))
                    nxb = I("v", "tensor_copy", x_bf[slot2][:], x_tok[slot][:],
                            waits=lwb, inc="dve")
                    pb0_bf = pb[0][:].bitcast(BF16)
                    tw = [("dve", nxb), ("dve", prev_tr_evac)]
                    if til == 0 and seg > 0:
                        tw.append(("dve", phaseB_end["dve"]))
                    if "xtr" not in SKIP:
                        for ai in range(8):
                            mm_ = I("t", "transpose",
                                    pb0_bf[:, 128 * ai:128 * (ai + 1)],
                                    x_bf[slot2][:, 128 * ai:128 * (ai + 1)],
                                    id128[:],
                                    waits=tw if ai == 0 else (),
                                    inc="pe" if ai == 7 else None)
                        last_xtr[(seg, til)] = mm_
                        tt_loc = til % NTIL
                        ew = [("pe", last_xtr[(seg, til)])]
                        if til == 0 and seg > 0:
                            ew += [("o0", C["o0"]), ("o1", C["o1"])]
                        prev_tr_evac = I(
                            "v", "tensor_copy",
                            xT[:, :, 128 * tt_loc:128 * (tt_loc + 1)],
                            pb0_bf.rearrange("p (a t) -> p a t", t=128),
                            waits=ew, inc="dve")

                SEC('qkproj')
                proj_guard = {0: (0, prev_r_done), 1: (0, prev_r_done)}
                last_de = prev_r_done
                for ht in ([] if "qk" in SKIP else range(H)):
                    par = ht % 2
                    pbank = pb[1 + par]
                    ga, gd = proj_guard[par]
                    mw = [("dve", prev_tr_evac), ("dve", gd)]
                    if ht < 2:
                        mw.append(("setup", wqk_loaded))
                    if ga:
                        mw.append(("act", ga))
                    for ai in range(8):
                        mm_ = I("t", "matmul", pbank[0:128, :],
                                wqk_sb[:, ai, ht, :],
                                xT[:, ai, :],
                                start=(ai == 0), stop=(ai == 7),
                                waits=mw if ai == 0 else (),
                                inc="pe" if ai == 7 else None)
                    mme = mm_
                    ae = I("a", "activation", exp_t[par][:],
                           pbank[0:128, :],
                           AF.Exp, bias=bqk_cols[:, ht:ht + 1], scale=1.0,
                           waits=[("pe", mme), ("dve", gd)], inc="act")
                    tsv = I("v", "tensor_scalar", t_relu[:],
                            pbank[0:128, :],
                            bqk_cols[:, ht:ht + 1], 0.0, op0=ALU.add,
                            op1=ALU.max,
                            waits=[("pe", mme), ("dve", last_de),
                                   ("act", ae)],
                            inc="dve")
                    dq = I("v", "scalar_tensor_tensor",
                           qT[:, ht, :], exp_t[par][0:64, :], 1.0,
                           t_relu[0:64, :],
                           op0=ALU.min, op1=ALU.add,
                           waits=[("act", ae), ("dve", tsv)], inc="dve")
                    de = I("v", "scalar_tensor_tensor",
                           kT[:, ht, :], exp_t[par][64:128, :], 1.0,
                           t_relu[64:128, :],
                           op0=ALU.min, op1=ALU.add, inc="dve")
                    proj_guard[par] = (ae, de)
                    last_de = de

                SEC('vproj')
                vaug_ready = {t_: 0 for t_ in range(NTIL)}
                prev_v_evac = 0
                for tt_loc in (() if "v" in SKIP else range(NTIL)):
                    mw = [("dve", vaug_init),
                          ("dve", proj_guard[0][1]), ("dve", proj_guard[1][1]),
                          ("act", proj_guard[0][0]), ("act", proj_guard[1][0])]
                    if prev_v_evac:
                        mw = [("dve", prev_v_evac)]
                    for hf in range(2):
                        pbank = pb[1 + hf]
                        for ai in range(8):
                            mm_ = I("t", "matmul", pbank[:],
                                    xT[:, ai, 128 * tt_loc:128 * (tt_loc + 1)],
                                    wv_sb[:, ai, 512 * hf:512 * (hf + 1)],
                                    start=(ai == 0), stop=(ai == 7),
                                    waits=mw if (ai == 0 and hf == 0) else (),
                                    inc="pe" if (ai == 7 and hf == 1) else None)
                    mme = mm_
                    va3 = v_aug[tt_loc][:].rearrange("p (h m) -> p h m", m=HD + 1)
                    src0 = pb[1][:].rearrange("p (h m) -> p h m", m=HD)
                    src1 = pb[2][:].rearrange("p (h m) -> p h m", m=HD)
                    bv3 = bv_bc[:].rearrange("p (h m) -> p h m", m=HD)
                    I("v", "tensor_tensor", va3[:, 0:8, 0:HD], src0,
                      bv3[:, 0:8, :], op=ALU.add, waits=[("pe", mme)])
                    prev_v_evac = I("v", "tensor_tensor", va3[:, 8:16, 0:HD], src1,
                                    bv3[:, 8:16, :], op=ALU.add, inc="dve")
                    vaug_ready[tt_loc] = prev_v_evac
                prev_r_done = prev_v_evac

                SEC('ktok')
                ktok_ready = {t_: 0 for t_ in range(NTIL)}
                for tt_loc in (() if "ktok" in SKIP else range(NTIL)):
                    pb0_bf = pb[0][:].bitcast(BF16)
                    tw = [("dve", proj_guard[1][1]), ("dve", prev_tr_evac)]
                    for h2 in range(H):
                        mm_ = I("t", "transpose", pb0_bf[:, HD * h2:HD * (h2 + 1)],
                                kT[0:64, h2, 128 * tt_loc:128 * (tt_loc + 1)],
                                id64[:],
                                waits=tw if h2 == 0 else (),
                                inc="pe" if h2 == H - 1 else None)
                    mme = mm_
                    prev_tr_evac = I("v", "tensor_copy", k_tok[tt_loc][:],
                                     pb0_bf.rearrange("p (a t) -> p a t", t=HD),
                                     waits=[("pe", mme)], inc="dve")
                    ktok_ready[tt_loc] = prev_tr_evac

                for cc in range(NTIL):
                    til = g * NTIL + cc
                    m34e = dele = oe = C["pe"]
                    SEC('afill')
                    maskev = [0, 0, 0, 0]
                    for bt in (() if "afill" in SKIP else range(4)):
                        par = bt % 2
                        pbank = pb[3 + par]
                        mw = [("dve", prev_mask[par]), ("dve", prev_Supd)]
                        for hh in range(4):
                            h2 = 4 * bt + hh
                            mm_ = I("t", "matmul", pbank[:, 128 * hh:128 * (hh + 1)],
                                    kT[0:64, h2, 128 * cc:128 * (cc + 1)],
                                    qT[0:64, h2, 128 * cc:128 * (cc + 1)],
                                    start=True, stop=True,
                                    waits=mw if hh == 0 else (),
                                    inc="pe" if hh == 3 else None)
                        mme = mm_
                        pa3 = pbank[:].rearrange("p (h t) -> p h t", t=128)
                        m3b = mask[:].unsqueeze(1).to_broadcast([128, 4, 128])
                        pt3 = PT[:].rearrange("p (h t) -> p h t", t=128)
                        maskev[bt] = I("v", "tensor_tensor",
                                       pt3[:, 4 * bt:4 * bt + 4, :], pa3, m3b,
                                       op=ALU.mult, waits=[("pe", mme)], inc="dve")
                        prev_mask[par] = maskev[bt]

                    SEC('m34')
                    def obank(h2):
                        if h2 < 6:
                            return 5, h2
                        if h2 < 12:
                            return 6, h2 - 6
                        return 7, h2 - 12
                    m34b = {}
                    for h2 in (() if "m34" in SKIP else range(H)):
                        bk2, sl = obank(h2)
                        off = sl * (HD + 1)
                        if h2 == 0:
                            mw = [("dve", prev_div), ("dve", prev_Supd),
                                  ("dve", vaug_ready[cc]), ("dve", maskev[0])]
                        elif h2 % 4 == 0:
                            mw = [("dve", maskev[h2 // 4])]
                        else:
                            mw = ()
                        I("t", "matmul", pb[bk2][:, off:off + HD + 1],
                          qT[0:64, h2, 128 * cc:128 * (cc + 1)],
                          S_bf[0:64, h2, :],
                          start=True, stop=False, waits=mw)
                        m34e = I("t", "matmul", pb[bk2][:, off:off + HD + 1],
                                 PT[:, 128 * h2:128 * (h2 + 1)],
                                 v_aug[cc][:, (HD + 1) * h2:(HD + 1) * (h2 + 1)],
                                 start=False, stop=True,
                                 inc="pe" if h2 in (5, 11, H - 1) else None)
                        if h2 in (5, 11, H - 1):
                            m34b[bk2] = m34e

                    SEC('div')
                    if "div" in SKIP:
                        cnts = []
                    else:
                        cnts = [(5, 6), (6, 6), (7, 4)]
                    offd = 0
                    for bk2, n2 in cnts:
                        denv = I("v", "tensor_scalar",
                                 den_sb[:, offd:offd + n2].unsqueeze(2),
                                 src_ap(pb[bk2], HD, [[512, 128], [HD + 1, n2], [1, 1]]),
                                 EPS_ATT, None, op0=ALU.add,
                                 waits=[("pe", m34b[bk2])],
                                 inc="dve" if bk2 == 7 else None)
                        offd += n2
                    if cnts:
                        rcv = I("v", "reciprocal", rcp_sb[:], den_sb[:],
                                waits=[("dve", denv)], inc="dve")
                    offd = 0
                    at3 = attn_tok[:].rearrange("p (h m) -> p h m", m=HD)
                    for bk2, n2 in cnts:
                        rc = rcp_sb[:, offd:offd + n2].unsqueeze(2).to_broadcast(
                            [128, n2, HD])
                        prev_div = I("v", "tensor_tensor",
                                     at3[:, offd:offd + n2, :],
                                     src_ap(pb[bk2], 0, [[512, 128], [HD + 1, n2], [1, HD]]),
                                     rc, op=ALU.mult,
                                     waits=[("dve", rcv)] if offd == 0 else (),
                                     inc="dve" if bk2 == 7 else None)
                        offd += n2

                    SEC('delta')
                    for h2 in (() if "delta" in SKIP else range(H)):
                        dbk = 5 + (1 if h2 >= 7 else 0) + (1 if h2 >= 14 else 0)
                        dsl = h2 - (0 if h2 < 7 else (7 if h2 < 14 else 14))
                        off = dsl * (HD + 1)
                        mw = [("dve", prev_div), ("dve", ktok_ready[cc])] \
                            if h2 == 0 else ()
                        dele = I("t", "matmul",
                                 pb[dbk][0:64, off:off + HD + 1],
                                 k_tok[cc][:, h2, :],
                                 v_aug[cc][:, (HD + 1) * h2:(HD + 1) * (h2 + 1)],
                                 start=True, stop=True, waits=mw,
                                 inc="pe" if h2 == H - 1 else None)

                    SEC('supd')
                    if "supd" not in SKIP:
                        dcnts = [(5, 0, 7), (6, 7, 7), (7, 14, 2)]
                        for dbk, h0, n2 in dcnts:
                            sadd = I("v", "tensor_tensor",
                                     S_f32[:, h0:h0 + n2, :],
                                     S_f32[:, h0:h0 + n2, :],
                                     pb[dbk][0:64, 0:n2 * (HD + 1)].rearrange(
                                         "p (q m) -> p q m", m=HD + 1),
                                     op=ALU.add,
                                     waits=[("pe", dele)] if dbk == 5 else (),
                                     inc="dve" if dbk == 7 else None)
                        prev_Supd = I("v", "tensor_copy", S_bf[:], S_f32[:],
                                      waits=[("dve", sadd)], inc="dve")

                    SEC('atr')
                    pb0_bf = pb[0][:].bitcast(BF16)
                    tw = [("dve", prev_div), ("dve", prev_tr_evac)]
                    for ai in (() if "atr" in SKIP else range(8)):
                        mm_ = I("t", "transpose", pb0_bf[:, 128 * ai:128 * (ai + 1)],
                                attn_tok[:, 128 * ai:128 * (ai + 1)], id128[:],
                                waits=tw if ai == 0 else (),
                                inc="pe" if ai == 7 else None)
                    if "atr" not in SKIP:
                        mme = mm_
                        prev_tr_evac = I("v", "tensor_copy", attnT[:],
                                         pb0_bf.rearrange("p (a t) -> p a t", t=128),
                                         waits=[("pe", mme)], inc="dve")
                    attnT_ev = prev_tr_evac

                    SEC('oproj')
                    mw = [("dve", attnT_ev), ("dve", prev_r_done)]
                    for hf in (() if "oproj" in SKIP else range(2)):
                        pbank = pb[1 + hf]
                        for ai in range(8):
                            I("t", "matmul", pbank[:],
                              attnT[:, ai, :],
                              wo_sb[:, ai, 512 * hf:512 * (hf + 1)],
                              start=(ai == 0), stop=False,
                              waits=mw if (ai == 0 and hf == 0) else ())
                        oe = I("t", "matmul", pbank[:], ones_bf[:],
                               bo_bf[0:1, 512 * hf:512 * (hf + 1)],
                               start=False, stop=True,
                               inc="pe" if hf == 1 else None)

                    SEC('ln1')
                    if "ln" in SKIP:
                        continue
                    rp = cc % 2
                    mw = [("pe", oe), (f"x{til % 4}", C[f"x{til % 4}"])]
                    if til == 0 and seg > 0:
                        mw += [("o0", C["o0"]), ("o1", C["o1"])]
                    if prev_y[rp]:
                        mw.append(("dve", prev_y[rp]))
                    I("v", "tensor_tensor", r_sb[rp][:, 0:512], pb[1][:],
                      x_tok[til % 4][:, 0:512], op=ALU.add, waits=mw)
                    rdone = I("v", "tensor_tensor", r_sb[rp][:, 512:1024],
                              pb[2][:], x_tok[til % 4][:, 512:1024],
                              op=ALU.add, inc="dve")
                    last_r[(seg, til)] = rdone
                    prev_r_done = rdone
                    rsv = I("v", "reduce_sum", colsA[:, 0:1], r_sb[rp][:],
                            axis=AX.X, waits=[("dve", rdone)], inc="dve")
                    sqe = I("a", "activation", sq_dump[:], r_sb[rp][:], AF.Square,
                            accum_out=colsA[:, 1:2], waits=[("dve", rdone)],
                            inc="act")
                    mnv = I("v", "tensor_scalar", colsA[:, 2:3], colsA[:, 0:1],
                            1.0 / D, None, op0=ALU.mult,
                            waits=[("dve", rsv)], inc="dve")
                    m2v = I("v", "tensor_scalar", colsA[:, 3:4], colsA[:, 2:3],
                            colsA[:, 2:3], None, op0=ALU.mult,
                            waits=[("dve", mnv)], inc="dve")
                    ve = I("v", "scalar_tensor_tensor", colsA[:, 4:5],
                           colsA[:, 1:2], 1.0 / D, colsA[:, 3:4],
                           op0=ALU.mult, op1=ALU.subtract,
                           waits=[("act", sqe), ("dve", m2v)], inc="dve")
                    lnv = I("a", "activation", colsA[:, 5:6], colsA[:, 4:5], AF.Ln,
                            bias=eps_col[:], waits=[("dve", ve)], inc="act")
                    re = I("a", "activation", colsA[:, 6:7], colsA[:, 5:6], AF.Exp,
                           scale=nhalf_col[:], waits=[("act", lnv)], inc="act")
                    prev_y[rp] = I("v", "tensor_scalar", y_tok[til][:],
                                   r_sb[rp][:], colsA[:, 2:3], colsA[:, 6:7],
                                   op0=ALU.subtract, op1=ALU.mult,
                                   waits=[("act", re)], inc="dve")
                    if MODE == "phaseA":
                        r0 = seg * SEG + til * 128
                        sp2 = til % 2
                        out_dma[sp2] = I("s", "dma_start",
                                         out_d[r0:r0 + 128, :], y_tok[til][:],
                                         waits=[("dve", prev_y[rp])],
                                         inc=f"o{sp2}")

            phaseA_end = {"pe": C["pe"], "dve": C["dve"], "act": C["act"]}
            if MODE == "phaseA":
                phaseB_end = {"pe": C["pe"], "dve": C["dve"], "act": C["act"]}
                continue

            gateB = [("pe", phaseA_end["pe"]), ("dve", phaseA_end["dve"]),
                     ("act", phaseA_end["act"])]

            SEC('prepass')
            prev_logit_reads = [0, 0]
            prev_ybf_tr = [0, 0]
            prev_tr_p = [prev_tr_evac, prev_tr_evac]
            prev_gte = 0
            ybfe_t = {}
            yT_ev_t = {}
            gte_t = {}
            def emit_tr(til):
                yp = til % 2
                mw = [("dve", prev_y[0]), ("dve", prev_y[1])] if til == 0 else []
                if prev_ybf_tr[yp]:
                    mw.append(("pe", prev_ybf_tr[yp]))
                ybfe_t[til] = I("v", "tensor_copy", y_bf[yp][:], y_tok[til][:],
                                waits=mw, inc="dve")
                pb0_bf = (pb[0] if yp == 0 else pb[5])[:].bitcast(BF16)
                tw = [("dve", ybfe_t[til]), ("dve", prev_tr_p[yp])]
                if til < 2:
                    tw += gateB
                for ai in range(8):
                    mm_ = I("t", "transpose", pb0_bf[:, 128 * ai:128 * (ai + 1)],
                            y_bf[yp][:, 128 * ai:128 * (ai + 1)], id128[:],
                            waits=tw if ai == 0 else (),
                            inc="pe" if ai == 7 else None)
                prev_ybf_tr[yp] = mm_
                prev_tr_p[yp] = I("v", "tensor_copy",
                                  yT[:, :, 128 * til:128 * (til + 1)],
                                  pb0_bf.rearrange("p (a t) -> p a t", t=128),
                                  waits=[("pe", mm_)], inc="dve")
                yT_ev_t[til] = prev_tr_p[yp]

            emit_tr(0)
            for til in range(8):
                yp = til % 2
                if til < 7:
                    emit_tr(til + 1)
                lbank = pb[1] if yp == 0 else pb[6]
                mw = [("dve", yT_ev_t[til])]
                if prev_logit_reads[yp]:
                    mw.append(("act", prev_logit_reads[yp]))
                if til < 2:
                    mw += gateB
                for ai in range(8):
                    lge = I("t", "matmul", lbank[:, 0:NE],
                            yT[:, ai, 128 * til:128 * (til + 1)], selT[:, ai, :],
                            start=(ai == 0), stop=(ai == 7),
                            waits=mw if ai == 0 else (),
                            inc="pe" if ai == 7 else None)
                sgw = [("pe", lge)]
                if prev_gte:
                    sgw.append(("dve", prev_gte))
                prev_logit_reads[yp] = I("a", "activation", s_sig[:],
                                         lbank[:, 0:NE],
                                         AF.Sigmoid, waits=sgw, inc="act")
                mxv = I("v", "max", out=m8[:], in_=s_sig[:],
                        waits=[("act", prev_logit_reads[yp])], inc="dve")
                msv = I("v", "memset", m8[:, 4:8], 0.0,
                        waits=[("dve", mxv)], inc="dve")
                mrv = I("v", "match_replace", out=zap[:], in_to_replace=m8[:],
                        in_values=s_sig[:], imm_value=0.0,
                        waits=[("dve", msv)], inc="dve")
                gte_t[til] = I("v", "tensor_sub",
                               gbuf_til[til][:], s_sig[:], zap[:],
                               waits=[("dve", mrv)], inc="dve")
                prev_gte = gte_t[til]
            prev_gt_evac = [0, 0]
            gt_done = {}
            for til in range(8):
                yp = til % 2
                pb3_bf = (pb[3] if yp == 0 else pb[7])[:].bitcast(BF16)
                gtw = [("dve", gte_t[til])]
                if prev_gt_evac[yp]:
                    gtw.append(("dve", prev_gt_evac[yp]))
                gt_done[til] = I("t", "transpose", pb3_bf[0:64, 0:128],
                                 gbuf_til[til][:],
                                 id128[:], waits=gtw, inc="pe")
                prev_gt_evac[yp] = I("v", "tensor_copy",
                                     G_seg[:, 128 * til:128 * (til + 1)],
                                     pb3_bf[0:64, 0:128],
                                     waits=[("pe", gt_done[til])], inc="dve")
            prev_tr_evac = prev_tr_p[0]
            gseg_done = prev_gt_evac[1]

            for w in range(NWIN):
                relu_done = {}
                SEC('hphase')
                mult_done = {}
                h_done = {}
                for e2 in range(NE):
                    par = e2 % 4
                    pb_par = e2 % 2
                    ksem = f"k{par}"
                    if e2 >= 4:
                        kw2 = [("pe", h_done[e2 - 4])]
                    elif w == 0:
                        kw2 = list(gateB)
                        if prev_h_tail is not None:
                            kw2.append(("pe", prev_h_tail[par]))
                    else:
                        kw2 = [("pe", prev_h_tail[par])]
                    nk = I("g", "dma_start", kbuf[par][:],
                           keys_d[e2].rearrange("(a p) c -> p a c", p=128),
                           waits=kw2, inc=ksem)
                    hbank = pb[[0, 1, 4, 5][e2 % 4]]
                    gbank = pb[[2, 3, 6, 7][e2 % 4]]
                    mw = [(ksem, nk)]
                    if e2 >= 4:
                        mw.append(("act", relu_done[e2 - 4]))
                    elif w == 0:
                        mw.append(("dve", gseg_done))
                    else:
                        mw.append(("dve", prev_flush_dve))
                    for ai in range(8):
                        mm_ = I("t", "matmul", hbank[:],
                                kbuf[par][:, ai, :],
                                yT[:, ai, 512 * w:512 * (w + 1)],
                                start=(ai == 0), stop=(ai == 7),
                                waits=mw if ai == 0 else (),
                                inc="pe" if ai == 7 else None)
                    h_done[e2] = mm_
                    gw = [("dve", mult_done[e2 - 4])] if e2 >= 4 else (
                        [("dve", gseg_done)] if w == 0 else
                        [("dve", prev_flush_dve)])
                    gbe = I("t", "matmul", gbank[:],
                            id64[:, e2:e2 + 1].to_broadcast([64, 128]),
                            G_seg[:, 512 * w:512 * (w + 1)],
                            start=True, stop=True, waits=gw, inc="pe")
                    rw = [("pe", h_done[e2])]
                    if e2 >= 2:
                        rw.append(("dve", mult_done[e2 - 2]))
                    relu_done[e2] = I("a", "activation", relu_sb[pb_par][:],
                                      hbank[:], AF.Relu,
                                      waits=rw, inc="act")
                    mult_done[e2] = I("v", "tensor_tensor", scores[:, e2, :],
                                      relu_sb[pb_par][:], gbank[:],
                                      op=ALU.mult,
                                      waits=[("act", relu_done[e2]),
                                             ("pe", gbe)], inc="dve")
                prev_h_tail = (h_done[NE - 4], h_done[NE - 3],
                               h_done[NE - 2], h_done[NE - 1])

                SEC('outphase')
                out_done = {}
                for e2 in range(NE):
                    par = e2 % 4
                    vsem = f"v{par}"
                    if e2 >= 4:
                        vw = [("pe", out_done[e2 - 4])]
                    elif w == 0:
                        vw = list(gateB)
                        if prev_out_tail is not None:
                            vw.append(("pe", prev_out_tail[par]))
                    else:
                        vw = [("pe", prev_out_tail[par])]
                    nv = I("g", "dma_start", vbuf[par][:], vals_d[e2],
                           waits=vw, inc=vsem)
                    mw = [(vsem, nv), ("dve", mult_done[e2])]
                    if e2 == 0:
                        mw.append(("act", relu_done[NE - 1]))
                        mw.append(("dve", mult_done[NE - 1]))
                    for st in range(4):
                        for vh in range(2):
                            mm_ = I("t", "matmul", pb[2 * st + vh][:],
                                    scores[:, e2, 128 * st:128 * (st + 1)],
                                    vbuf[par][:, 512 * vh:512 * (vh + 1)],
                                    start=(e2 == 0), stop=(e2 == NE - 1),
                                    waits=mw if (st == 0 and vh == 0) else (),
                                    inc="pe" if (st == 3 and vh == 1) else None)
                    out_done[e2] = mm_
                prev_out_tail = (out_done[NE - 4], out_done[NE - 3],
                                 out_done[NE - 2], out_done[NE - 1])
                outs_done = out_done[NE - 1]

                SEC('flush')
                for st in range(4):
                    til = 4 * w + st
                    sp = st % 2
                    mw = [("pe", outs_done)]
                    I("v", "tensor_tensor", r2_sb[sp][:, 0:512], pb[2 * st][:],
                      y_tok[til][:, 0:512], op=ALU.add, waits=mw)
                    rde = I("v", "tensor_tensor", r2_sb[sp][:, 512:1024],
                            pb[2 * st + 1][:], y_tok[til][:, 512:1024],
                            op=ALU.add, inc="dve")
                    rsv = I("v", "reduce_sum", colsA[:, 0:1], r2_sb[sp][:],
                            axis=AX.X, waits=[("dve", rde)], inc="dve")
                    sqe = I("a", "activation", sq_dump[:], r2_sb[sp][:],
                            AF.Square, accum_out=colsA[:, 1:2],
                            waits=[("dve", rde)], inc="act")
                    mnv = I("v", "tensor_scalar", colsA[:, 2:3], colsA[:, 0:1],
                            1.0 / D, None, op0=ALU.mult,
                            waits=[("dve", rsv)], inc="dve")
                    m2v = I("v", "tensor_scalar", colsA[:, 3:4], colsA[:, 2:3],
                            colsA[:, 2:3], None, op0=ALU.mult,
                            waits=[("dve", mnv)], inc="dve")
                    ve = I("v", "scalar_tensor_tensor", colsA[:, 4:5],
                           colsA[:, 1:2], 1.0 / D, colsA[:, 3:4],
                           op0=ALU.mult, op1=ALU.subtract,
                           waits=[("act", sqe), ("dve", m2v)], inc="dve")
                    lnv = I("a", "activation", colsA[:, 5:6], colsA[:, 4:5],
                            AF.Ln, bias=eps_col[:], waits=[("dve", ve)],
                            inc="act")
                    re = I("a", "activation", colsA[:, 6:7], colsA[:, 5:6],
                           AF.Exp, scale=nhalf_col[:], waits=[("act", lnv)],
                           inc="act")
                    yw = [("act", re)]
                    if out_dma[sp]:
                        yw.append((f"o{sp}", out_dma[sp]))
                    ye = I("v", "tensor_scalar", out_f32[sp][:], r2_sb[sp][:],
                           colsA[:, 2:3], colsA[:, 6:7],
                           op0=ALU.subtract, op1=ALU.mult,
                           waits=yw, inc="dve")
                    r0 = seg * SEG + til * 128
                    out_dma[sp] = I("s", "dma_start", out_d[r0:r0 + 128, :],
                                    out_f32[sp][:], waits=[("dve", ye)],
                                    inc=f"o{sp}")
                    prev_flush_dve = ye

            phaseB_end = {"pe": C["pe"], "dve": C["dve"], "act": C["act"]}

        W("s", [("o0", C["o0"]), ("o1", C["o1"])])

    with nc.Block() as blk:
        @blk.tensor
        def _(t):
            flow(t, "t")

        @blk.vector
        def _(v):
            flow(v, "v")

        @blk.scalar
        def _(a2):
            flow(a2, "a")

        @blk.sync
        def _(s):
            flow(s, "s")

        @blk.gpsimd
        def _(g):
            flow(g, "g")

    ctx.close()
    return nc


_PROG_CACHE = {}


def kernel(L0_attn_x, L0_attn_mask, L0_attn_Wq, L0_attn_bq, L0_attn_Wk,
           L0_attn_bk, L0_attn_Wv, L0_attn_bv, L0_attn_Wo, L0_attn_bo,
           L0_ffn_expert_sel, L0_ffn_keys, L0_ffn_values):
    x = np.ascontiguousarray(np.asarray(L0_attn_x, dtype=np.float32))
    B, S, Dm = x.shape
    assert (B, S, Dm) == (16, 2048, 1024)

    if "nc" not in _PROG_CACHE:
        _PROG_CACHE["nc"] = build_program()
    nc = _PROG_CACHE["nc"]

    weights = {
        "wq": np.ascontiguousarray(L0_attn_Wq, dtype=np.float32),
        "wk": np.ascontiguousarray(L0_attn_Wk, dtype=np.float32),
        "wv": np.ascontiguousarray(L0_attn_Wv, dtype=np.float32),
        "wo": np.ascontiguousarray(L0_attn_Wo, dtype=np.float32),
        "bq": np.ascontiguousarray(L0_attn_bq, dtype=np.float32).reshape(1, D),
        "bk": np.ascontiguousarray(L0_attn_bk, dtype=np.float32).reshape(1, D),
        "bv": np.ascontiguousarray(L0_attn_bv, dtype=np.float32).reshape(1, D),
        "bo": np.ascontiguousarray(L0_attn_bo, dtype=np.float32).reshape(1, D),
        "sel": np.ascontiguousarray(L0_ffn_expert_sel, dtype=np.float32),
        "keys": np.ascontiguousarray(L0_ffn_keys, dtype=np.float32),
        "vals": np.ascontiguousarray(L0_ffn_values, dtype=np.float32),
    }
    in_maps = []
    for c in range(NCORE):
        m = dict(weights)
        m["x"] = np.ascontiguousarray(x[2 * c:2 * c + 2].reshape(4096, D))
        in_maps.append(m)

    res = run_bass_kernel_spmd(nc, in_maps, core_ids=list(range(NCORE)))
    out = np.concatenate(
        [r["out"].reshape(2, S, Dm) for r in res.results], axis=0)
    return out


# version 16
# speedup vs baseline: 1.0182x; 1.0182x over previous
import numpy as np
import ml_dtypes
from contextlib import ExitStack

import concourse.bass as bass
import concourse.mybir as mybir
from concourse.bass_utils import run_bass_kernel_spmd

F32 = mybir.dt.float32
BF16 = mybir.dt.bfloat16
AF = mybir.ActivationFunctionType
ALU = mybir.AluOpType
AX = mybir.AxisListType

NCORE = 8
D = 1024
H = 16
HD = 64
NSEG = 4
SEG = 1024
NGRP = 2
GRP = 512
NTIL = 4
NE = 64
EC = 128
NWIN = 2
WIN = 512
EPS_LN = 1e-5
EPS_ATT = 1e-6
MODE = "full"
SKIP = set()
SECMAP = {}
_CURSEC = [""]


def build_program():
    nc = bass.Bass("TRN2", target_bir_lowering=False)

    x_d = nc.dram_tensor("x", [4096, D], F32, kind="ExternalInput").ap()
    wq_d = nc.dram_tensor("wq", [D, D], F32, kind="ExternalInput").ap()
    wk_d = nc.dram_tensor("wk", [D, D], F32, kind="ExternalInput").ap()
    wv_d = nc.dram_tensor("wv", [D, D], F32, kind="ExternalInput").ap()
    wo_d = nc.dram_tensor("wo", [D, D], F32, kind="ExternalInput").ap()
    bq_d = nc.dram_tensor("bq", [1, D], F32, kind="ExternalInput").ap()
    bk_d = nc.dram_tensor("bk", [1, D], F32, kind="ExternalInput").ap()
    bv_d = nc.dram_tensor("bv", [1, D], F32, kind="ExternalInput").ap()
    bo_d = nc.dram_tensor("bo", [1, D], F32, kind="ExternalInput").ap()
    sel_d = nc.dram_tensor("sel", [NE, D], F32, kind="ExternalInput").ap()
    keys_d = nc.dram_tensor("keys", [NE, D, EC], F32, kind="ExternalInput").ap()
    vals_d = nc.dram_tensor("vals", [NE, EC, D], F32, kind="ExternalInput").ap()
    out_d = nc.dram_tensor("out", [4096, D], F32, kind="ExternalOutput").ap()

    id128_np = np.eye(128, dtype=ml_dtypes.bfloat16)
    id64_np = np.eye(64, dtype=ml_dtypes.bfloat16)
    mask_np = np.triu(np.ones((128, 128), dtype=np.float32))
    id128_d = nc.inline_tensor(id128_np, name="id128c").ap()
    id64_d = nc.inline_tensor(id64_np, name="id64c").ap()
    mask_d = nc.inline_tensor(mask_np, name="maskc").ap()

    ctx = ExitStack()
    e = ctx.enter_context

    id128 = e(nc.sbuf_tensor("id128", [128, 128], BF16))
    id64 = e(nc.sbuf_tensor("id64", [64, 64], BF16))
    mask = e(nc.sbuf_tensor("mask", [128, 128], F32))
    selT = e(nc.sbuf_tensor("selT", [128, 8, NE], BF16))
    ones_row = e(nc.sbuf_tensor("ones_row", [1, 128], F32))
    ones_bf = e(nc.sbuf_tensor("ones_bf", [1, 128], BF16))
    bo_bf = e(nc.sbuf_tensor("bo_bf", [1, D], BF16))
    one1 = e(nc.sbuf_tensor("one1", [1, 1], F32))
    bo_row = e(nc.sbuf_tensor("bo_row", [1, D], F32))
    sq_dump = e(nc.sbuf_tensor("sq_dump", [128, D], BF16))
    bq_cols = e(nc.sbuf_tensor("bq_cols", [64, 16], F32))
    bk_cols = e(nc.sbuf_tensor("bk_cols", [64, 16], F32))
    bqk_cols = e(nc.sbuf_tensor("bqk_cols", [128, 16], F32))
    bv_bc = e(nc.sbuf_tensor("bv_bc", [128, D], F32))
    wv_sb = e(nc.sbuf_tensor("wv_sb", [128, 8, D], BF16))
    wo_sb = e(nc.sbuf_tensor("wo_sb", [128, 8, D], BF16))
    S_f32 = e(nc.sbuf_tensor("S_f32", [64, H, HD + 1], F32))
    S_bf = e(nc.sbuf_tensor("S_bf", [64, H, HD + 1], BF16))
    y_tok = [e(nc.sbuf_tensor(f"y_tok{i}", [128, D], F32)) for i in range(8)]
    colsA = e(nc.sbuf_tensor("colsA", [128, 16], F32))
    eps_col = e(nc.sbuf_tensor("eps_col", [128, 1], F32))
    nhalf_col = e(nc.sbuf_tensor("nhalf_col", [128, 1], F32))
    den_sb = e(nc.sbuf_tensor("den_sb", [128, 16], F32))
    rcp_sb = e(nc.sbuf_tensor("rcp_sb", [128, 16], F32))

    pb = [e(nc.psum_tensor(f"pb{i}", [128, 512], F32)) for i in range(8)]

    actxW = ExitStack()
    wqk_sb = actxW.enter_context(
        nc.sbuf_tensor("wqk_sb", [128, 8, H, 128], BF16))

    actxA = ExitStack()
    a = actxA.enter_context
    actxS = ExitStack()
    s_ = actxS.enter_context
    b_rows = s_(nc.sbuf_tensor("b_rows", [1, 3 * D], F32))
    sel_bf = s_(nc.sbuf_tensor("sel_bf", [NE, D], BF16))
    actxS.close()
    x_tok = [a(nc.sbuf_tensor(f"x_tok{i}", [128, D], F32)) for i in range(4)]
    x_bf = [a(nc.sbuf_tensor(f"x_bf{i}", [128, D], BF16)) for i in range(2)]
    qT = a(nc.sbuf_tensor("qT", [64, H, GRP], BF16))
    kT = a(nc.sbuf_tensor("kT", [64, H, GRP], BF16))
    v_aug = [a(nc.sbuf_tensor(f"v_aug{i}", [128, H * (HD + 1)], BF16)) for i in range(4)]
    k_tok = [a(nc.sbuf_tensor(f"k_tok{i}", [128, H, HD], BF16)) for i in range(4)]
    exp_t = [a(nc.sbuf_tensor(f"exp_t{i}", [128, GRP], F32)) for i in range(2)]
    t_relu = a(nc.sbuf_tensor("t_relu", [128, GRP], F32))
    r_sb = [a(nc.sbuf_tensor(f"r_sb{i}", [128, D], F32)) for i in range(2)]
    actxA1 = ExitStack()
    xT = actxA1.enter_context(nc.sbuf_tensor("xT", [128, 8, GRP], BF16))
    actxA1.close()
    actxA2 = ExitStack()
    a2_ = actxA2.enter_context
    PT = a2_(nc.sbuf_tensor("PT", [128, H * 128], BF16))
    attn_tok = a2_(nc.sbuf_tensor("attn_tok", [128, H * HD], BF16))
    attnT = a2_(nc.sbuf_tensor("attnT", [128, 8, 128], BF16))
    actxA2.close()
    actxA.close()
    actxW.close()

    actxB = ExitStack()
    b = actxB.enter_context
    yT = b(nc.sbuf_tensor("yT", [128, 8, SEG], BF16))
    G_seg = b(nc.sbuf_tensor("G_seg", [NE, SEG], BF16))
    scores = b(nc.sbuf_tensor("scores", [128, NE, WIN], BF16))
    kbuf = [b(nc.sbuf_tensor(f"kbuf{i}", [128, 8, EC], BF16)) for i in range(4)]
    vbuf = [b(nc.sbuf_tensor(f"vbuf{i}", [128, D], BF16)) for i in range(4)]
    relu_sb = [b(nc.sbuf_tensor(f"relu_sb{i}", [128, WIN], BF16)) for i in range(2)]
    y_bf = [b(nc.sbuf_tensor(f"y_bf{i}", [128, D], BF16)) for i in range(2)]
    s_sig = b(nc.sbuf_tensor("s_sig", [128, NE], F32))
    m8 = b(nc.sbuf_tensor("m8", [128, 8], F32))
    zap = b(nc.sbuf_tensor("zap", [128, NE], F32))
    gates_bf = b(nc.sbuf_tensor("gates_bf", [128, NE], BF16))
    r2_sb = [b(nc.sbuf_tensor(f"r2_sb{i}", [128, D], F32)) for i in range(2)]
    out_f32 = [b(nc.sbuf_tensor(f"out_f32{i}", [128, D], F32)) for i in range(2)]
    actxB.close()

    sem_names = (["setup", "setup2", "pe", "dve", "act",
                  "xb0", "xb1", "o0", "o1"]
                 + [f"x{i}" for i in range(4)]
                 + [f"k{i}" for i in range(4)] + [f"v{i}" for i in range(4)])
    sems = {n: ctx.enter_context(nc.semaphore(f"sem_{n}")) for n in sem_names}
    C = {n: 0 for n in sem_names}

    def flow(eng, E):
        for k in C:
            C[k] = 0

        def I(ek, method, *args, waits=(), inc=None, incn=1, **kw):
            val = None
            if inc is not None:
                if method == "dma_start":
                    incn = 16
                C[inc] += incn
                val = C[inc]
            if E == ek:
                for s2, v2 in waits:
                    eng.wait_ge(sems[s2], v2)
                ins = getattr(eng, method)(*args, **kw)
                if inc is not None:
                    ins.then_inc(sems[inc], incn)
                try:
                    SECMAP[ins.ins.name] = _CURSEC[0]
                except Exception:
                    pass
            return val

        def SEC(name):
            _CURSEC[0] = name

        def W(ek, waits):
            if E == ek and waits:
                for s2, v2 in waits:
                    eng.wait_ge(sems[s2], v2)

        def DR(ek):
            if E == ek:
                eng.drain()

        def src_ap(handle, off, ap):
            return bass.AP(handle, off, ap)

        SEC('setup')
        I("g", "dma_start", id128[:], id128_d, inc="setup")
        I("g", "dma_start", id64[:], id64_d, inc="setup")
        I("g", "dma_start", mask[:], mask_d, inc="setup")
        I("g", "dma_start", sel_bf[:], sel_d, inc="setup")
        for ai in range(8):
            I("g", "dma_start", wqk_sb[:, ai, :, 0:64],
              wq_d.rearrange("(a p) (h c) -> p a h c", p=128, c=64)[:, ai],
              inc="setup")
            I("g", "dma_start", wqk_sb[:, ai, :, 64:128],
              wk_d.rearrange("(a p) (h c) -> p a h c", p=128, c=64)[:, ai],
              inc="setup")
        I("g", "dma_start", wv_sb[:], wv_d.rearrange("(a p) m -> p a m", p=128), inc="setup")
        I("g", "dma_start", wo_sb[:], wo_d.rearrange("(a p) m -> p a m", p=128), inc="setup")
        I("s", "dma_start", b_rows[:, 0:D], bq_d, inc="setup2")
        I("s", "dma_start", b_rows[:, D:2 * D], bk_d, inc="setup2")
        I("s", "dma_start", b_rows[:, 2 * D:3 * D], bv_d, inc="setup2")
        I("s", "dma_start", bo_row[:], bo_d, inc="setup2")
        n_setup = C["setup"]
        n_setup2 = C["setup2"]

        I("v", "memset", ones_row[:], 1.0)
        I("v", "memset", ones_bf[:], 1.0)
        I("v", "memset", eps_col[:], EPS_LN)
        I("v", "memset", nhalf_col[:], -0.5)
        I("v", "memset", one1[:], 1.0)
        I("v", "memset", S_f32[:], 0.0)
        I("v", "memset", S_bf[:], 0.0, inc="dve")
        dve_memset = C["dve"]

        setup_w = [("setup", n_setup), ("setup2", n_setup2), ("dve", dve_memset)]
        for bi, off in [(0, 0), (1, D)]:
            for ai in range(16):
                I("t", "matmul", pb[bi][0:64, ai:ai + 1],
                  b_rows[0:1, off + 64 * ai:off + 64 * (ai + 1)], one1[:],
                  start=True, stop=True,
                  waits=setup_w if (ai == 0 and bi == 0) else ())
        for hf in range(2):
            I("t", "matmul", pb[2 + hf][:, 0:512],
              ones_row[:], b_rows[0:1, 2 * D + 512 * hf:2 * D + 512 * (hf + 1)],
              start=True, stop=True)
        pb4_bf = pb[4][:].bitcast(BF16)
        for ai in range(8):
            I("t", "transpose", pb4_bf[0:128, 128 * ai:128 * ai + 64],
              sel_bf[:, 128 * ai:128 * (ai + 1)], id64[:],
              inc="pe" if ai == 7 else None)
        pe_setup = C["pe"]
        I("v", "tensor_copy", bo_bf[:], bo_row[:],
          waits=[("setup2", n_setup2)])
        I("v", "tensor_copy", bqk_cols[0:64, :], pb[0][0:64, 0:16],
          waits=[("pe", pe_setup)])
        I("v", "tensor_copy", bqk_cols[64:128, :], pb[1][0:64, 0:16])
        I("v", "tensor_copy", bv_bc[:, 0:512], pb[2][:, 0:512])
        I("v", "tensor_copy", bv_bc[:, 512:1024], pb[3][:, 0:512])
        I("v", "tensor_copy", selT[:],
          pb4_bf.rearrange("p (a e) -> p a e", e=128)[:, :, 0:NE], inc="dve")
        dve_setup = C["dve"]

        last_r = {}
        last_xtr = {}
        prev_div = 0
        prev_Supd = dve_setup
        prev_tr_evac = dve_setup
        prev_mask = [dve_setup, dve_setup]
        prev_r_done = dve_setup
        prev_y = [0, 0]
        phaseB_end = None
        prev_flush_dve = 0
        prev_h_tail = None
        prev_out_tail = None
        out_dma = [0, 0]

        for seg in range(NSEG):
            if seg == 0:
                gateA = [("setup", n_setup), ("setup2", n_setup2), ("dve", dve_setup)]
            else:
                gateA = [("pe", phaseB_end["pe"]), ("dve", phaseB_end["dve"]),
                         ("act", phaseB_end["act"]),
                         ("o0", C["o0"]), ("o1", C["o1"])]
                for ai in range(8):
                    I("g", "dma_start", wqk_sb[:, ai, :, 0:64],
                      wq_d.rearrange("(a p) (h c) -> p a h c",
                                     p=128, c=64)[:, ai],
                      waits=[("pe", prev_out_tail[1])] if ai == 0 else (),
                      inc="setup")
                    I("g", "dma_start", wqk_sb[:, ai, :, 64:128],
                      wk_d.rearrange("(a p) (h c) -> p a h c",
                                     p=128, c=64)[:, ai],
                      waits=[("pe", prev_out_tail[1])] if ai == 0 else (),
                      inc="setup")
            wqk_loaded = C["setup"]

            for i2 in range(4):
                I("v", "memset",
                  v_aug[i2][:].rearrange("p (h m) -> p h m", m=HD + 1)[:, :, HD:HD + 1],
                  1.0,
                  waits=(gateA if seg == 0 else [("pe", phaseB_end["pe"])])
                  if i2 == 0 else ())
            if seg % 2 == 0:
                I("v", "memset", S_f32[:], 0.0)
            vaug_init = I("v", "memset",
                          S_bf[:] if seg % 2 == 0 else S_bf[0:1, 0, 0:1],
                          0.0, inc="dve")

            gateX = gateA if seg == 0 else [("pe", phaseB_end["pe"])]
            for g in range(NGRP):
                tiles = [g * NTIL + i2 for i2 in range(NTIL)]
                SEC('xload')
                for til in tiles:
                    slot = til % 4
                    pk = (seg, til - NTIL) if g != 0 else (seg - 1, til + NTIL)
                    lw = list(gateX)
                    if pk in last_r:
                        lw.append(("dve", last_r[pk]))
                    r0 = seg * SEG + til * 128
                    nx = I("s" if til % 2 == 0 else "g", "dma_start",
                           x_tok[slot][:], x_d[r0:r0 + 128, :],
                           waits=lw, inc=f"x{slot}")
                    slot2 = til % 2
                    pk2 = (seg, til - 2) if til >= 2 else (seg - 1, til + 6)
                    lwb = [(f"x{slot}", nx)]
                    if pk2 in last_xtr:
                        lwb.append(("pe", last_xtr[pk2]))
                    nxb = I("v", "tensor_copy", x_bf[slot2][:], x_tok[slot][:],
                            waits=lwb, inc="dve")
                    pb0_bf = pb[0][:].bitcast(BF16)
                    tw = [("dve", nxb), ("dve", prev_tr_evac)]
                    if til == 0 and seg > 0:
                        tw.append(("dve", phaseB_end["dve"]))
                    if "xtr" not in SKIP:
                        for ai in range(8):
                            mm_ = I("t", "transpose",
                                    pb0_bf[:, 128 * ai:128 * (ai + 1)],
                                    x_bf[slot2][:, 128 * ai:128 * (ai + 1)],
                                    id128[:],
                                    waits=tw if ai == 0 else (),
                                    inc="pe" if ai == 7 else None)
                        last_xtr[(seg, til)] = mm_
                        tt_loc = til % NTIL
                        ew = [("pe", last_xtr[(seg, til)])]
                        if til == 0 and seg > 0:
                            ew += [("o0", C["o0"]), ("o1", C["o1"])]
                        prev_tr_evac = I(
                            "v", "tensor_copy",
                            xT[:, :, 128 * tt_loc:128 * (tt_loc + 1)],
                            pb0_bf.rearrange("p (a t) -> p a t", t=128),
                            waits=ew, inc="dve")

                SEC('qkproj')
                proj_guard = {0: (0, prev_r_done), 1: (0, prev_r_done)}
                last_de = prev_r_done
                for ht in ([] if "qk" in SKIP else range(H)):
                    par = ht % 2
                    pbank = pb[1 + par]
                    ga, gd = proj_guard[par]
                    mw = [("dve", prev_tr_evac), ("dve", gd)]
                    if ht < 2:
                        mw.append(("setup", wqk_loaded))
                    if ga:
                        mw.append(("act", ga))
                    for ai in range(8):
                        mm_ = I("t", "matmul", pbank[0:128, :],
                                wqk_sb[:, ai, ht, :],
                                xT[:, ai, :],
                                start=(ai == 0), stop=(ai == 7),
                                waits=mw if ai == 0 else (),
                                inc="pe" if ai == 7 else None)
                    mme = mm_
                    ae = I("a", "activation", exp_t[par][:],
                           pbank[0:128, :],
                           AF.Exp, bias=bqk_cols[:, ht:ht + 1], scale=1.0,
                           waits=[("pe", mme), ("dve", gd)], inc="act")
                    tsv = I("v", "tensor_scalar", t_relu[:],
                            pbank[0:128, :],
                            bqk_cols[:, ht:ht + 1], 0.0, op0=ALU.add,
                            op1=ALU.max,
                            waits=[("pe", mme), ("dve", last_de),
                                   ("act", ae)],
                            inc="dve")
                    dq = I("v", "scalar_tensor_tensor",
                           qT[:, ht, :], exp_t[par][0:64, :], 1.0,
                           t_relu[0:64, :],
                           op0=ALU.min, op1=ALU.add,
                           waits=[("act", ae), ("dve", tsv)], inc="dve")
                    de = I("v", "scalar_tensor_tensor",
                           kT[:, ht, :], exp_t[par][64:128, :], 1.0,
                           t_relu[64:128, :],
                           op0=ALU.min, op1=ALU.add, inc="dve")
                    proj_guard[par] = (ae, de)
                    last_de = de

                SEC('vproj')
                vaug_ready = {t_: 0 for t_ in range(NTIL)}
                prev_v_evac = 0
                for tt_loc in (() if "v" in SKIP else range(NTIL)):
                    mw = [("dve", vaug_init),
                          ("dve", proj_guard[0][1]), ("dve", proj_guard[1][1]),
                          ("act", proj_guard[0][0]), ("act", proj_guard[1][0])]
                    if prev_v_evac:
                        mw = [("dve", prev_v_evac)]
                    for hf in range(2):
                        pbank = pb[1 + hf]
                        for ai in range(8):
                            mm_ = I("t", "matmul", pbank[:],
                                    xT[:, ai, 128 * tt_loc:128 * (tt_loc + 1)],
                                    wv_sb[:, ai, 512 * hf:512 * (hf + 1)],
                                    start=(ai == 0), stop=(ai == 7),
                                    waits=mw if (ai == 0 and hf == 0) else (),
                                    inc="pe" if (ai == 7 and hf == 1) else None)
                    mme = mm_
                    va3 = v_aug[tt_loc][:].rearrange("p (h m) -> p h m", m=HD + 1)
                    src0 = pb[1][:].rearrange("p (h m) -> p h m", m=HD)
                    src1 = pb[2][:].rearrange("p (h m) -> p h m", m=HD)
                    bv3 = bv_bc[:].rearrange("p (h m) -> p h m", m=HD)
                    I("v", "tensor_tensor", va3[:, 0:8, 0:HD], src0,
                      bv3[:, 0:8, :], op=ALU.add, waits=[("pe", mme)])
                    prev_v_evac = I("v", "tensor_tensor", va3[:, 8:16, 0:HD], src1,
                                    bv3[:, 8:16, :], op=ALU.add, inc="dve")
                    vaug_ready[tt_loc] = prev_v_evac
                prev_r_done = prev_v_evac

                SEC('ktok')
                ktok_ready = {t_: 0 for t_ in range(NTIL)}
                for tt_loc in (() if "ktok" in SKIP else range(NTIL)):
                    pb0_bf = pb[0][:].bitcast(BF16)
                    tw = [("dve", proj_guard[1][1]), ("dve", prev_tr_evac)]
                    for h2 in range(H):
                        mm_ = I("t", "transpose", pb0_bf[:, HD * h2:HD * (h2 + 1)],
                                kT[0:64, h2, 128 * tt_loc:128 * (tt_loc + 1)],
                                id64[:],
                                waits=tw if h2 == 0 else (),
                                inc="pe" if h2 == H - 1 else None)
                    mme = mm_
                    prev_tr_evac = I("v", "tensor_copy", k_tok[tt_loc][:],
                                     pb0_bf.rearrange("p (a t) -> p a t", t=HD),
                                     waits=[("pe", mme)], inc="dve")
                    ktok_ready[tt_loc] = prev_tr_evac

                for cc in range(NTIL):
                    til = g * NTIL + cc
                    m34e = dele = oe = C["pe"]
                    SEC('afill')
                    maskev = [0, 0, 0, 0]
                    for bt in (() if "afill" in SKIP else range(4)):
                        par = bt % 2
                        pbank = pb[3 + par]
                        mw = [("dve", prev_mask[par])]
                        if seg == 0 and g == 0 and cc == 0:
                            mw.append(("dve", dve_setup))
                        for hh in range(4):
                            h2 = 4 * bt + hh
                            mm_ = I("t", "matmul", pbank[:, 128 * hh:128 * (hh + 1)],
                                    kT[0:64, h2, 128 * cc:128 * (cc + 1)],
                                    qT[0:64, h2, 128 * cc:128 * (cc + 1)],
                                    start=True, stop=True,
                                    waits=mw if hh == 0 else (),
                                    inc="pe" if hh == 3 else None)
                        mme = mm_
                        pa3 = pbank[:].rearrange("p (h t) -> p h t", t=128)
                        m3b = mask[:].unsqueeze(1).to_broadcast([128, 4, 128])
                        pt3 = PT[:].rearrange("p (h t) -> p h t", t=128)
                        maskev[bt] = I("v", "tensor_tensor",
                                       pt3[:, 4 * bt:4 * bt + 4, :], pa3, m3b,
                                       op=ALU.mult, waits=[("pe", mme)], inc="dve")
                        prev_mask[par] = maskev[bt]

                    SEC('m34')
                    def obank(h2):
                        if h2 < 6:
                            return 5, h2
                        if h2 < 12:
                            return 6, h2 - 6
                        return 7, h2 - 12
                    m34b = {}
                    for h2 in (() if "m34" in SKIP else range(H)):
                        bk2, sl = obank(h2)
                        off = sl * (HD + 1)
                        if h2 == 0:
                            mw = [("dve", prev_div), ("dve", prev_Supd),
                                  ("dve", vaug_ready[cc]), ("dve", maskev[0])]
                        elif h2 % 4 == 0:
                            mw = [("dve", maskev[h2 // 4])]
                        else:
                            mw = ()
                        I("t", "matmul", pb[bk2][:, off:off + HD + 1],
                          qT[0:64, h2, 128 * cc:128 * (cc + 1)],
                          S_bf[0:64, h2, :],
                          start=True, stop=False, waits=mw)
                        m34e = I("t", "matmul", pb[bk2][:, off:off + HD + 1],
                                 PT[:, 128 * h2:128 * (h2 + 1)],
                                 v_aug[cc][:, (HD + 1) * h2:(HD + 1) * (h2 + 1)],
                                 start=False, stop=True,
                                 inc="pe" if h2 in (5, 11, H - 1) else None)
                        if h2 in (5, 11, H - 1):
                            m34b[bk2] = m34e

                    SEC('div')
                    if "div" in SKIP:
                        cnts = []
                    else:
                        cnts = [(5, 6), (6, 6), (7, 4)]
                    offd = 0
                    for bk2, n2 in cnts:
                        denv = I("v", "tensor_scalar",
                                 den_sb[:, offd:offd + n2].unsqueeze(2),
                                 src_ap(pb[bk2], HD, [[512, 128], [HD + 1, n2], [1, 1]]),
                                 EPS_ATT, None, op0=ALU.add,
                                 waits=[("pe", m34b[bk2])],
                                 inc="dve" if bk2 == 7 else None)
                        offd += n2
                    if cnts:
                        rcv = I("v", "reciprocal", rcp_sb[:], den_sb[:],
                                waits=[("dve", denv)], inc="dve")
                    offd = 0
                    at3 = attn_tok[:].rearrange("p (h m) -> p h m", m=HD)
                    for bk2, n2 in cnts:
                        rc = rcp_sb[:, offd:offd + n2].unsqueeze(2).to_broadcast(
                            [128, n2, HD])
                        prev_div = I("v", "tensor_tensor",
                                     at3[:, offd:offd + n2, :],
                                     src_ap(pb[bk2], 0, [[512, 128], [HD + 1, n2], [1, HD]]),
                                     rc, op=ALU.mult,
                                     waits=[("dve", rcv)] if offd == 0 else (),
                                     inc="dve" if bk2 == 7 else None)
                        offd += n2

                    SEC('delta')
                    for h2 in (() if "delta" in SKIP else range(H)):
                        dbk = 5 + (1 if h2 >= 7 else 0) + (1 if h2 >= 14 else 0)
                        dsl = h2 - (0 if h2 < 7 else (7 if h2 < 14 else 14))
                        off = dsl * (HD + 1)
                        mw = [("dve", prev_div), ("dve", ktok_ready[cc])] \
                            if h2 == 0 else ()
                        dele = I("t", "matmul",
                                 pb[dbk][0:64, off:off + HD + 1],
                                 k_tok[cc][:, h2, :],
                                 v_aug[cc][:, (HD + 1) * h2:(HD + 1) * (h2 + 1)],
                                 start=True, stop=True, waits=mw,
                                 inc="pe" if h2 == H - 1 else None)

                    SEC('supd')
                    if "supd" not in SKIP:
                        dcnts = [(5, 0, 7), (6, 7, 7), (7, 14, 2)]
                        for dbk, h0, n2 in dcnts:
                            sadd = I("v", "tensor_tensor",
                                     S_f32[:, h0:h0 + n2, :],
                                     S_f32[:, h0:h0 + n2, :],
                                     pb[dbk][0:64, 0:n2 * (HD + 1)].rearrange(
                                         "p (q m) -> p q m", m=HD + 1),
                                     op=ALU.add,
                                     waits=[("pe", dele)] if dbk == 5 else (),
                                     inc="dve" if dbk == 7 else None)
                        prev_Supd = I("v", "tensor_copy", S_bf[:], S_f32[:],
                                      waits=[("dve", sadd)], inc="dve")

                    SEC('atr')
                    pb0_bf = pb[0][:].bitcast(BF16)
                    tw = [("dve", prev_div), ("dve", prev_tr_evac)]
                    for ai in (() if "atr" in SKIP else range(8)):
                        mm_ = I("t", "transpose", pb0_bf[:, 128 * ai:128 * (ai + 1)],
                                attn_tok[:, 128 * ai:128 * (ai + 1)], id128[:],
                                waits=tw if ai == 0 else (),
                                inc="pe" if ai == 7 else None)
                    if "atr" not in SKIP:
                        mme = mm_
                        prev_tr_evac = I("v", "tensor_copy", attnT[:],
                                         pb0_bf.rearrange("p (a t) -> p a t", t=128),
                                         waits=[("pe", mme)], inc="dve")
                    attnT_ev = prev_tr_evac

                    SEC('oproj')
                    mw = [("dve", attnT_ev), ("dve", prev_r_done)]
                    for hf in (() if "oproj" in SKIP else range(2)):
                        pbank = pb[1 + hf]
                        for ai in range(8):
                            I("t", "matmul", pbank[:],
                              attnT[:, ai, :],
                              wo_sb[:, ai, 512 * hf:512 * (hf + 1)],
                              start=(ai == 0), stop=False,
                              waits=mw if (ai == 0 and hf == 0) else ())
                        oe = I("t", "matmul", pbank[:], ones_bf[:],
                               bo_bf[0:1, 512 * hf:512 * (hf + 1)],
                               start=False, stop=True,
                               inc="pe" if hf == 1 else None)

                    SEC('ln1')
                    if "ln" in SKIP:
                        continue
                    rp = cc % 2
                    mw = [("pe", oe), (f"x{til % 4}", C[f"x{til % 4}"])]
                    if til == 0 and seg > 0:
                        mw += [("o0", C["o0"]), ("o1", C["o1"])]
                    if prev_y[rp]:
                        mw.append(("dve", prev_y[rp]))
                    I("v", "tensor_tensor", r_sb[rp][:, 0:512], pb[1][:],
                      x_tok[til % 4][:, 0:512], op=ALU.add, waits=mw)
                    rdone = I("v", "tensor_tensor", r_sb[rp][:, 512:1024],
                              pb[2][:], x_tok[til % 4][:, 512:1024],
                              op=ALU.add, inc="dve")
                    last_r[(seg, til)] = rdone
                    prev_r_done = rdone
                    rsv = I("v", "reduce_sum", colsA[:, 0:1], r_sb[rp][:],
                            axis=AX.X, waits=[("dve", rdone)], inc="dve")
                    sqe = I("a", "activation", sq_dump[:], r_sb[rp][:], AF.Square,
                            accum_out=colsA[:, 1:2], waits=[("dve", rdone)],
                            inc="act")
                    mnv = I("v", "tensor_scalar", colsA[:, 2:3], colsA[:, 0:1],
                            1.0 / D, None, op0=ALU.mult,
                            waits=[("dve", rsv)], inc="dve")
                    m2v = I("v", "tensor_scalar", colsA[:, 3:4], colsA[:, 2:3],
                            colsA[:, 2:3], None, op0=ALU.mult,
                            waits=[("dve", mnv)], inc="dve")
                    ve = I("v", "scalar_tensor_tensor", colsA[:, 4:5],
                           colsA[:, 1:2], 1.0 / D, colsA[:, 3:4],
                           op0=ALU.mult, op1=ALU.subtract,
                           waits=[("act", sqe), ("dve", m2v)], inc="dve")
                    lnv = I("a", "activation", colsA[:, 5:6], colsA[:, 4:5], AF.Ln,
                            bias=eps_col[:], waits=[("dve", ve)], inc="act")
                    re = I("a", "activation", colsA[:, 6:7], colsA[:, 5:6], AF.Exp,
                           scale=nhalf_col[:], waits=[("act", lnv)], inc="act")
                    prev_y[rp] = I("v", "tensor_scalar", y_tok[til][:],
                                   r_sb[rp][:], colsA[:, 2:3], colsA[:, 6:7],
                                   op0=ALU.subtract, op1=ALU.mult,
                                   waits=[("act", re)], inc="dve")
                    if MODE == "phaseA":
                        r0 = seg * SEG + til * 128
                        sp2 = til % 2
                        out_dma[sp2] = I("s", "dma_start",
                                         out_d[r0:r0 + 128, :], y_tok[til][:],
                                         waits=[("dve", prev_y[rp])],
                                         inc=f"o{sp2}")

            phaseA_end = {"pe": C["pe"], "dve": C["dve"], "act": C["act"]}
            if MODE == "phaseA":
                phaseB_end = {"pe": C["pe"], "dve": C["dve"], "act": C["act"]}
                continue

            gateB = [("pe", phaseA_end["pe"]), ("dve", phaseA_end["dve"]),
                     ("act", phaseA_end["act"])]

            SEC('prepass')
            prev_gt_evac = [0, 0]
            prev_logit_reads = [0, 0]
            prev_ybf_tr = [0, 0]
            prev_tr_p = [prev_tr_evac, prev_tr_evac]
            prev_gte = 0
            for til in range(8):
                yp = til % 2
                mw = [("dve", prev_y[0]), ("dve", prev_y[1])] if til == 0 else []
                if prev_ybf_tr[yp]:
                    mw.append(("pe", prev_ybf_tr[yp]))
                ybfe = I("v", "tensor_copy", y_bf[yp][:], y_tok[til][:],
                         waits=mw, inc="dve")
                pb0_bf = (pb[0] if yp == 0 else pb[5])[:].bitcast(BF16)
                tw = [("dve", ybfe), ("dve", prev_tr_p[yp])]
                if til < 2:
                    tw += gateB
                for ai in range(8):
                    mm_ = I("t", "transpose", pb0_bf[:, 128 * ai:128 * (ai + 1)],
                            y_bf[yp][:, 128 * ai:128 * (ai + 1)], id128[:],
                            waits=tw if ai == 0 else (),
                            inc="pe" if ai == 7 else None)
                mme = mm_
                prev_ybf_tr[yp] = mme
                prev_tr_p[yp] = I("v", "tensor_copy",
                                  yT[:, :, 128 * til:128 * (til + 1)],
                                  pb0_bf.rearrange("p (a t) -> p a t", t=128),
                                  waits=[("pe", mme)], inc="dve")
                yT_ev = prev_tr_p[yp]
                lbank = pb[1] if yp == 0 else pb[6]
                mw = [("dve", yT_ev)]
                if prev_logit_reads[yp]:
                    mw.append(("act", prev_logit_reads[yp]))
                if til < 2:
                    mw += gateB
                for ai in range(8):
                    lge = I("t", "matmul", lbank[:, 0:NE],
                            yT[:, ai, 128 * til:128 * (til + 1)], selT[:, ai, :],
                            start=(ai == 0), stop=(ai == 7),
                            waits=mw if ai == 0 else (),
                            inc="pe" if ai == 7 else None)
                sgw = [("pe", lge)]
                if prev_gte:
                    sgw.append(("dve", prev_gte))
                prev_logit_reads[yp] = I("a", "activation", s_sig[:],
                                         lbank[:, 0:NE],
                                         AF.Sigmoid, waits=sgw, inc="act")
                mxv = I("v", "max", out=m8[:], in_=s_sig[:],
                        waits=[("act", prev_logit_reads[yp])], inc="dve")
                msv = I("v", "memset", m8[:, 4:8], 0.0,
                        waits=[("dve", mxv)], inc="dve")
                mrv = I("v", "match_replace", out=zap[:], in_to_replace=m8[:],
                        in_values=s_sig[:], imm_value=0.0,
                        waits=[("dve", msv)], inc="dve")
                gte = I("v", "tensor_sub", gates_bf[:], s_sig[:], zap[:],
                        waits=[("dve", mrv)], inc="dve")
                prev_gte = gte
                pb3_bf = (pb[3] if yp == 0 else pb[7])[:].bitcast(BF16)
                gtw = [("dve", gte)]
                if prev_gt_evac[yp]:
                    gtw.append(("dve", prev_gt_evac[yp]))
                mme = I("t", "transpose", pb3_bf[0:64, 0:128], gates_bf[:],
                        id128[:],
                        waits=gtw, inc="pe")
                prev_gt_evac[yp] = I("v", "tensor_copy",
                                     G_seg[:, 128 * til:128 * (til + 1)],
                                     pb3_bf[0:64, 0:128],
                                     waits=[("pe", mme)], inc="dve")
            prev_tr_evac = prev_tr_p[0]
            gseg_done = prev_gt_evac[1]

            for w in range(NWIN):
                relu_done = {}
                SEC('hphase')
                mult_done = {}
                h_done = {}
                for e2 in range(NE):
                    par = e2 % 4
                    pb_par = e2 % 2
                    ksem = f"k{par}"
                    if e2 >= 4:
                        kw2 = [("pe", h_done[e2 - 4])]
                    elif w == 0:
                        kw2 = list(gateB)
                        if prev_h_tail is not None:
                            kw2.append(("pe", prev_h_tail[par]))
                    else:
                        kw2 = [("pe", prev_h_tail[par])]
                    nk = I("g", "dma_start", kbuf[par][:],
                           keys_d[e2].rearrange("(a p) c -> p a c", p=128),
                           waits=kw2, inc=ksem)
                    hbank = pb[[0, 1, 4, 5][e2 % 4]]
                    gbank = pb[[2, 3, 6, 7][e2 % 4]]
                    mw = [(ksem, nk)]
                    if e2 >= 4:
                        mw.append(("act", relu_done[e2 - 4]))
                    elif w == 0:
                        mw.append(("dve", gseg_done))
                    else:
                        mw.append(("dve", prev_flush_dve))
                    for ai in range(8):
                        mm_ = I("t", "matmul", hbank[:],
                                kbuf[par][:, ai, :],
                                yT[:, ai, 512 * w:512 * (w + 1)],
                                start=(ai == 0), stop=(ai == 7),
                                waits=mw if ai == 0 else (),
                                inc="pe" if ai == 7 else None)
                    h_done[e2] = mm_
                    gw = [("dve", mult_done[e2 - 4])] if e2 >= 4 else (
                        [("dve", gseg_done)] if w == 0 else
                        [("dve", prev_flush_dve)])
                    gbe = I("t", "matmul", gbank[:],
                            id64[:, e2:e2 + 1].to_broadcast([64, 128]),
                            G_seg[:, 512 * w:512 * (w + 1)],
                            start=True, stop=True, waits=gw, inc="pe")
                    rw = [("pe", h_done[e2])]
                    if e2 >= 2:
                        rw.append(("dve", mult_done[e2 - 2]))
                    relu_done[e2] = I("a", "activation", relu_sb[pb_par][:],
                                      hbank[:], AF.Relu,
                                      waits=rw, inc="act")
                    mult_done[e2] = I("v", "tensor_tensor", scores[:, e2, :],
                                      relu_sb[pb_par][:], gbank[:],
                                      op=ALU.mult,
                                      waits=[("act", relu_done[e2]),
                                             ("pe", gbe)], inc="dve")
                prev_h_tail = (h_done[NE - 4], h_done[NE - 3],
                               h_done[NE - 2], h_done[NE - 1])

                SEC('outphase')
                out_done = {}
                for e2 in range(NE):
                    par = e2 % 4
                    vsem = f"v{par}"
                    if e2 >= 4:
                        vw = [("pe", out_done[e2 - 4])]
                    elif w == 0:
                        vw = list(gateB)
                        if prev_out_tail is not None:
                            vw.append(("pe", prev_out_tail[par]))
                    else:
                        vw = [("pe", prev_out_tail[par])]
                    nv = I("g", "dma_start", vbuf[par][:], vals_d[e2],
                           waits=vw, inc=vsem)
                    mw = [(vsem, nv), ("dve", mult_done[e2])]
                    if e2 == 0:
                        mw.append(("act", relu_done[NE - 1]))
                        mw.append(("dve", mult_done[NE - 1]))
                    for st in range(4):
                        for vh in range(2):
                            mm_ = I("t", "matmul", pb[2 * st + vh][:],
                                    scores[:, e2, 128 * st:128 * (st + 1)],
                                    vbuf[par][:, 512 * vh:512 * (vh + 1)],
                                    start=(e2 == 0), stop=(e2 == NE - 1),
                                    waits=mw if (st == 0 and vh == 0) else (),
                                    inc="pe" if (st == 3 and vh == 1) else None)
                    out_done[e2] = mm_
                prev_out_tail = (out_done[NE - 4], out_done[NE - 3],
                                 out_done[NE - 2], out_done[NE - 1])
                outs_done = out_done[NE - 1]

                SEC('flush')
                for st in range(4):
                    til = 4 * w + st
                    sp = st % 2
                    mw = [("pe", outs_done)]
                    I("v", "tensor_tensor", r2_sb[sp][:, 0:512], pb[2 * st][:],
                      y_tok[til][:, 0:512], op=ALU.add, waits=mw)
                    rde = I("v", "tensor_tensor", r2_sb[sp][:, 512:1024],
                            pb[2 * st + 1][:], y_tok[til][:, 512:1024],
                            op=ALU.add, inc="dve")
                    rsv = I("v", "reduce_sum", colsA[:, 0:1], r2_sb[sp][:],
                            axis=AX.X, waits=[("dve", rde)], inc="dve")
                    sqe = I("a", "activation", sq_dump[:], r2_sb[sp][:],
                            AF.Square, accum_out=colsA[:, 1:2],
                            waits=[("dve", rde)], inc="act")
                    mnv = I("v", "tensor_scalar", colsA[:, 2:3], colsA[:, 0:1],
                            1.0 / D, None, op0=ALU.mult,
                            waits=[("dve", rsv)], inc="dve")
                    m2v = I("v", "tensor_scalar", colsA[:, 3:4], colsA[:, 2:3],
                            colsA[:, 2:3], None, op0=ALU.mult,
                            waits=[("dve", mnv)], inc="dve")
                    ve = I("v", "scalar_tensor_tensor", colsA[:, 4:5],
                           colsA[:, 1:2], 1.0 / D, colsA[:, 3:4],
                           op0=ALU.mult, op1=ALU.subtract,
                           waits=[("act", sqe), ("dve", m2v)], inc="dve")
                    lnv = I("a", "activation", colsA[:, 5:6], colsA[:, 4:5],
                            AF.Ln, bias=eps_col[:], waits=[("dve", ve)],
                            inc="act")
                    re = I("a", "activation", colsA[:, 6:7], colsA[:, 5:6],
                           AF.Exp, scale=nhalf_col[:], waits=[("act", lnv)],
                           inc="act")
                    yw = [("act", re)]
                    if out_dma[sp]:
                        yw.append((f"o{sp}", out_dma[sp]))
                    ye = I("v", "tensor_scalar", out_f32[sp][:], r2_sb[sp][:],
                           colsA[:, 2:3], colsA[:, 6:7],
                           op0=ALU.subtract, op1=ALU.mult,
                           waits=yw, inc="dve")
                    r0 = seg * SEG + til * 128
                    out_dma[sp] = I("s", "dma_start", out_d[r0:r0 + 128, :],
                                    out_f32[sp][:], waits=[("dve", ye)],
                                    inc=f"o{sp}")
                    prev_flush_dve = ye

            phaseB_end = {"pe": C["pe"], "dve": C["dve"], "act": C["act"]}

        W("s", [("o0", C["o0"]), ("o1", C["o1"])])

    with nc.Block() as blk:
        @blk.tensor
        def _(t):
            flow(t, "t")

        @blk.vector
        def _(v):
            flow(v, "v")

        @blk.scalar
        def _(a2):
            flow(a2, "a")

        @blk.sync
        def _(s):
            flow(s, "s")

        @blk.gpsimd
        def _(g):
            flow(g, "g")

    ctx.close()
    return nc


_PROG_CACHE = {}


def kernel(L0_attn_x, L0_attn_mask, L0_attn_Wq, L0_attn_bq, L0_attn_Wk,
           L0_attn_bk, L0_attn_Wv, L0_attn_bv, L0_attn_Wo, L0_attn_bo,
           L0_ffn_expert_sel, L0_ffn_keys, L0_ffn_values):
    x = np.ascontiguousarray(np.asarray(L0_attn_x, dtype=np.float32))
    B, S, Dm = x.shape
    assert (B, S, Dm) == (16, 2048, 1024)

    if "nc" not in _PROG_CACHE:
        _PROG_CACHE["nc"] = build_program()
    nc = _PROG_CACHE["nc"]

    weights = {
        "wq": np.ascontiguousarray(L0_attn_Wq, dtype=np.float32),
        "wk": np.ascontiguousarray(L0_attn_Wk, dtype=np.float32),
        "wv": np.ascontiguousarray(L0_attn_Wv, dtype=np.float32),
        "wo": np.ascontiguousarray(L0_attn_Wo, dtype=np.float32),
        "bq": np.ascontiguousarray(L0_attn_bq, dtype=np.float32).reshape(1, D),
        "bk": np.ascontiguousarray(L0_attn_bk, dtype=np.float32).reshape(1, D),
        "bv": np.ascontiguousarray(L0_attn_bv, dtype=np.float32).reshape(1, D),
        "bo": np.ascontiguousarray(L0_attn_bo, dtype=np.float32).reshape(1, D),
        "sel": np.ascontiguousarray(L0_ffn_expert_sel, dtype=np.float32),
        "keys": np.ascontiguousarray(L0_ffn_keys, dtype=np.float32),
        "vals": np.ascontiguousarray(L0_ffn_values, dtype=np.float32),
    }
    in_maps = []
    for c in range(NCORE):
        m = dict(weights)
        m["x"] = np.ascontiguousarray(x[2 * c:2 * c + 2].reshape(4096, D))
        in_maps.append(m)

    res = run_bass_kernel_spmd(nc, in_maps, core_ids=list(range(NCORE)))
    out = np.concatenate(
        [r["out"].reshape(2, S, Dm) for r in res.results], axis=0)
    return out
